# version 21
# speedup vs baseline: 2.3410x; 1.0012x over previous
import numpy as np
import ml_dtypes
import concourse.bass as bass
import concourse.mybir as mybir

F32 = mybir.dt.float32
BF16 = mybir.dt.bfloat16
AF = mybir.ActivationFunctionType
ALU = mybir.AluOpType
AX = mybir.AxisListType
from concourse.bass_utils import run_bass_kernel_spmd as __run_spmd
from contextlib import ExitStack







class Prog:

    ENGINES = ("sync", "vector", "scalar", "tensor", "gpsimd")

    def __init__(self, nc, mode="serial", plan=None):
        self.nc = nc
        self.mode = mode
        self.plan = plan
        self.ops = []
        self.meta = []

    def op(self, engine, fn, inc=1):
        self.ops.append((engine, fn, inc))

    def dma(self, out, in_, engine="sync", **kw):
        self.op(engine, lambda e: e.dma_start(out=out, in_=in_, **kw), 16)

    def emit(self, block, S):
        def runner(engine_name, eng):
            cnt = 0
            for (en, fn, inc) in self.ops:
                if en == engine_name:
                    eng.wait_ge(S, cnt)
                    fn(eng).then_inc(S, inc)
                cnt += inc
            if engine_name == "sync":
                eng.wait_ge(S, cnt)
        self._run_sections(block, runner)

    def _run_sections(self, block, runner):
        @block.sync
        def _(sync):
            runner("sync", sync)
        @block.vector
        def _(vector):
            runner("vector", vector)
        @block.scalar
        def _(scalar):
            runner("scalar", scalar)
        @block.tensor
        def _(tensor):
            runner("tensor", tensor)
        @block.gpsimd
        def _(gpsimd):
            runner("gpsimd", gpsimd)

    @staticmethod
    def _pap_region(pap):
        esize = {F32: 4, BF16: 2}.get(pap.dtype, 4)
        return (pap.memref, pap.offset, tuple(tuple(x) for x in pap.ap), esize)

    def emit_harvest(self, block):
        meta = self.meta = [None] * len(self.ops)

        def runner(engine_name, eng):
            for idx, (en, fn, inc) in enumerate(self.ops):
                if en == engine_name:
                    inst = fn(eng)
                    ii = inst.ins
                    reads = [self._pap_region(a) for a in ii.ins
                             if hasattr(a, "memref")]
                    writes = [self._pap_region(a) for a in ii.outs
                              if hasattr(a, "memref")]
                    meta[idx] = (reads, writes)
        self._run_sections(block, runner)

    @staticmethod
    def compute_plan(ops, meta, alloc):
        PSUM_BANK = 2048
        def key_range(reg):
            name, off, appairs, esize = reg
            sp, addr, size = alloc.get(name, ("?", None, None))
            if addr is None or sp not in ("SB", "PSUM"):
                span = 1
                for (stride, num) in appairs:
                    span += (num - 1) * abs(stride)
                return ("T:" + name, off * esize, (off + span) * esize)
            rowstride = appairs[0][0] if appairs else 1
            if rowstride <= 0:
                rowstride = 1
            col = off % rowstride if rowstride > 1 else off
            span = 1
            for (stride, num) in appairs[1:]:
                span += (num - 1) * abs(stride)
            if not appairs[1:]:
                span = 1
            a0 = addr + col * esize
            a1 = addr + (col + span) * esize
            if sp == "PSUM":
                a0 = a0 // PSUM_BANK * PSUM_BANK
                a1 = -(-a1 // PSUM_BANK) * PSUM_BANK
            return (sp, a0, a1)
        NDQ = Prog.NDQ
        recs = {}
        counts = {e: 0 for e in Prog.ENGINES}
        kdma = 0
        plan = []
        for idx, (en, fn, inc) in enumerate(ops):
            reads, writes = meta[idx]
            if en == "sync":
                selfkey = ("dq", kdma % NDQ)
                selfval = 16 * (kdma // NDQ + 1)
            else:
                selfkey = en
                selfval = counts[en] + inc
            need = {}
            def visit(regs, want_writes_only):
                for reg in regs:
                    sp, a0, a1 = key_range(reg)
                    lst = recs.get(sp)
                    if not lst:
                        continue
                    ispsum = (sp == "PSUM")
                    for (rlo, rhi, rk, rc, rw) in lst:
                        if rlo < a1 and a0 < rhi:
                            if en == "tensor" and rk == "tensor":
                                continue
                            if en == "sync" and rk == selfkey:
                                continue
                            if rw or not want_writes_only or ispsum:
                                if need.get(rk, 0) < rc:
                                    need[rk] = rc
            visit(reads, True)
            visit(writes, False)
            for (regs, is_w) in ((writes, True), (reads, False)):
                for reg in regs:
                    sp, a0, a1 = key_range(reg)
                    lst = recs.setdefault(sp, [])
                    if is_w:
                        lst[:] = [r for r in lst
                                  if not (a0 <= r[0] and r[1] <= a1)]
                    lst.append((a0, a1, selfkey, selfval, is_w))
                    if len(lst) > 48:
                        lo0 = min(r[0] for r in lst[:16])
                        hi0 = max(r[1] for r in lst[:16])
                        emax = {}
                        for r in lst[:16]:
                            emax[r[2]] = max(emax.get(r[2], 0), r[3])
                        merged = [(lo0, hi0, e2, c2, True) for e2, c2 in emax.items()]
                        lst[:] = merged + lst[16:]
            if en == "sync":
                kdma += 1
            else:
                counts[en] += inc
            plan.append(need)
        return plan

    NDQ = 24

    @staticmethod
    def serialize_plan(ops):
        NDQ = Prog.NDQ
        cur = {}
        counts = {e: 0 for e in Prog.ENGINES}
        kdma = 0
        plan = []
        for idx, (en, fn, inc) in enumerate(ops):
            plan.append(dict(cur))
            if en == "sync":
                cur[("dq", kdma % NDQ)] = 16 * (kdma // NDQ + 1)
                kdma += 1
            else:
                counts[en] += inc
                cur[en] = counts[en]
        return plan

    @staticmethod
    def finalize_plan(ops, plan):
        ENG = Prog.ENGINES
        NDQ = Prog.NDQ
        dma_ord = {}
        k = 0
        for idx, (en, fn, inc) in enumerate(ops):
            if en == "sync":
                dma_ord[idx] = k
                k += 1
        counts = {e: 0 for e in ENG}
        wm = {e: {} for e in ENG}
        waits_per_op = []
        waited = {e: set() for e in ENG}
        for idx, (en, fn, inc) in enumerate(ops):
            lst = []
            for key, val in plan[idx].items():
                if wm[en].get(key, 0) < val:
                    lst.append((key, val))
                    wm[en][key] = val
                    if not isinstance(key, tuple):
                        waited[key].add(val)
            waits_per_op.append(lst)
            counts[en] += inc
        for e in ENG:
            if counts[e] and e != "sync":
                waited[e].add(counts[e])
        inc_per_op = [0] * len(ops)
        cum = {e: 0 for e in ENG}
        pend = {e: 0 for e in ENG}
        wsorted = {e: sorted(waited[e]) for e in ENG}
        wi = {e: 0 for e in ENG}
        dq_of_op = [None] * len(ops)
        for idx, (en, fn, inc) in enumerate(ops):
            if en == "sync":
                kk = dma_ord[idx]
                selfw = 16 * (kk // NDQ) if kk >= NDQ else 0
                dq_of_op[idx] = (kk % NDQ, selfw)
                inc_per_op[idx] = 16
                continue
            cum[en] += inc
            pend[en] += inc
            ws = wsorted[en]
            i = wi[en]
            if i < len(ws) and cum[en] >= ws[i]:
                inc_per_op[idx] = pend[en]
                pend[en] = 0
                while i < len(ws) and cum[en] >= ws[i]:
                    i += 1
                wi[en] = i
        return waits_per_op, inc_per_op, dq_of_op

    def emit_sched(self, block, sems, dqsems):
        waits_per_op, inc_per_op, dq_of_op = self.plan

        def sem_of(key):
            if isinstance(key, tuple):
                return dqsems[key[1]]
            return sems[key]

        totals = {e: 0 for e in self.ENGINES}
        dqtot = [0] * self.NDQ
        kk = 0
        for (en2, fn2, inc2) in self.ops:
            if en2 == "sync":
                dqtot[kk % self.NDQ] += 16
                kk += 1
            else:
                totals[en2] += inc2

        def runner(engine_name, eng):
            for idx, (en, fn, inc) in enumerate(self.ops):
                if en == engine_name:
                    if en == "sync":
                        j, selfw = dq_of_op[idx]
                        if selfw:
                            eng.wait_ge(dqsems[j], selfw)
                    for (key, v) in waits_per_op[idx]:
                        eng.wait_ge(sem_of(key), v)
                    inst = fn(eng)
                    if en == "sync":
                        inst.then_inc(dqsems[dq_of_op[idx][0]], 16)
                    elif inc_per_op[idx]:
                        inst.then_inc(sems[en], inc_per_op[idx])
            for e2 in self.ENGINES:
                if e2 != "sync" and totals[e2]:
                    eng.wait_ge(sems[e2], totals[e2])
            for j in range(self.NDQ):
                if dqtot[j]:
                    eng.wait_ge(dqsems[j], dqtot[j])
        self._run_sections(block, runner)


def extract_alloc_map(nc):
    import json
    j = json.loads(nc.to_json_str())
    out = {}
    def walk(d):
        if isinstance(d, dict):
            if d.get("Skind") == "memory_location" and "addr" in d:
                name = d.get("name")
                sp = d.get("type")
                dims = d.get("dims", [1, 1])
                size = dims[-1] if len(dims) >= 1 else 0
                out[name] = (sp, d.get("addr", 0), d.get("dims", [0, 0])[-1])
            for v in d.values():
                walk(v)
        elif isinstance(d, list):
            for v in d:
                walk(v)
    walk(j)
    return out


def load_weight_bf16(p, staging, dst_bf, src_dram, kc, fout):
    stg_cols = staging.shape[1]
    for c in range(kc):
        off = 0
        while off < fout:
            n = min(stg_cols, fout - off)
            p.dma(staging[:, 0:n], src_dram[c, :, off:off + n])
            d = dst_bf[:, c * fout + off: c * fout + off + n]
            s = staging[:, 0:n]
            p.op("vector", lambda e, d=d, s=s: e.tensor_copy(d, s))
            off += n


def host_constants():
    ident = np.eye(128, dtype=np.float32)
    ident_bf = np.eye(128, dtype=ml_dtypes.bfloat16)
    tril = (np.arange(128)[:, None] <= np.arange(128)[None, :]).astype(np.float32)
    trilmask_bf = tril.astype(ml_dtypes.bfloat16)
    ones_col = np.ones((128, 1), dtype=np.float32)
    ones_row_bf = np.ones((1, 128), dtype=ml_dtypes.bfloat16)
    ones_row = np.ones((1, 128), dtype=np.float32)
    sel_heads = np.zeros((32, 2048), dtype=np.float32)
    for j in range(16):
        sel_heads[2 * j, j * 128: j * 128 + 64] = 1.0
        sel_heads[2 * j + 1, j * 128 + 64: j * 128 + 128] = 1.0
    sel16 = np.zeros((16, 16 * 128), dtype=ml_dtypes.bfloat16)
    for n in range(16):
        sel16[n, n * 128:(n + 1) * 128] = 1.0
    sel8 = np.zeros((8, 8 * 128), dtype=ml_dtypes.bfloat16)
    for n in range(8):
        sel8[n, n * 128:(n + 1) * 128] = 1.0
    e127 = np.zeros((128, 128), dtype=np.float32)
    e127[127, :] = 1.0
    ones128_bf = np.ones((128, 1), dtype=ml_dtypes.bfloat16)
    return dict(ident=ident, ident_bf=ident_bf, tril=tril, trilmask_bf=trilmask_bf,
                ones_col=ones_col, ones_row=ones_row, ones_row_bf=ones_row_bf,
                sel_heads=sel_heads, sel16=sel16, sel8=sel8, e127=e127,
                ones128_bf=ones128_bf)


def alloc_postln(nc, es, prefix=""):
    t = {}
    sb = lambda name, shape, dt: es.enter_context(nc.sbuf_tensor(prefix + name, shape, dt))
    ps = lambda name, shape, dt=F32: es.enter_context(nc.psum_tensor(prefix + name, shape, dt))
    t["sq"] = sb("ln_sq", [128, 512], BF16)
    t["rows"] = sb("ln_rows", [1, 4 * 512], F32)
    t["rows_bf"] = sb("ln_rows_bf", [1, 2 * 512], BF16)
    t["rstdb"] = sb("ln_rstdb", [128, 512], BF16)
    t["eps"] = sb("ln_eps", [128, 1], F32)
    t["xout32"] = sb("ln_xout32", [128, 512], F32)
    return t


def emit_postln2(p, t, orow, ocol, rchunk, outchunk, nchunk, T, final_fp32, d_model=1024):
    for c in range(nchunk):
        p.op("tensor", lambda e, c=c: e.matmul(
            t["pstat0"][:, 0:T], ocol[:, 0:1], rchunk(c),
            start=(c == 0), stop=(c == nchunk - 1)))
    for c in range(nchunk):
        p.op("scalar", lambda e, c=c: e.activation(
            t["sq"][:, 0:T], rchunk(c), AF.Square))
        p.op("tensor", lambda e, c=c: e.matmul(
            t["pstat1"][:, 0:T], ocol[:, 0:1], t["sq"][:, 0:T],
            start=(c == 0), stop=(c == nchunk - 1)))
    inv = 1.0 / d_model
    R = t["rows"]
    p.op("vector", lambda e: e.tensor_scalar(R[:, 0:T], t["pstat0"][:, 0:T], inv, None, ALU.mult))
    p.op("vector", lambda e: e.tensor_scalar(R[:, 512:512 + T], t["pstat1"][:, 0:T], inv, None, ALU.mult))
    p.op("vector", lambda e: e.tensor_tensor(R[:, 1024:1024 + T], R[:, 0:T], R[:, 0:T], ALU.mult))
    p.op("vector", lambda e: e.tensor_tensor(R[:, 1024:1024 + T], R[:, 512:512 + T], R[:, 1024:1024 + T], ALU.subtract))
    p.op("scalar", lambda e: e.activation(R[:, 1536:1536 + T], R[:, 1024:1024 + T], AF.Ln, bias=t["eps"][0:1, :]))
    p.op("scalar", lambda e: e.activation(R[:, 1536:1536 + T], R[:, 1536:1536 + T], AF.Exp, scale=-0.5))
    p.op("vector", lambda e: e.tensor_copy(t["rows_bf"][:, 0:T], R[:, 0:T]))
    p.op("vector", lambda e: e.tensor_copy(t["rows_bf"][:, 512:512 + T], R[:, 1536:1536 + T]))
    p.op("tensor", lambda e: e.matmul(t["pmub"][:, 0:T], orow[:, 0:128], t["rows_bf"][:, 0:T], start=True, stop=True))
    p.op("tensor", lambda e: e.matmul(t["pbr"][:, 0:T], orow[:, 0:128], t["rows_bf"][:, 512:512 + T], start=True, stop=True))
    p.op("scalar", lambda e: e.copy(t["rstdb"][:, 0:T], t["pbr"][:, 0:T]))
    for c in range(nchunk):
        if final_fp32:
            p.op("vector", lambda e, c=c: e.tensor_tensor(
                t["xout32"][:, 0:T], rchunk(c), t["pmub"][:, 0:T], ALU.subtract))
            p.op("vector", lambda e, c=c: e.tensor_tensor(
                t["xout32"][:, 0:T], t["xout32"][:, 0:T], t["rstdb"][:, 0:T], ALU.mult))
            p.dma(outchunk(c), t["xout32"][:, 0:T])
        else:
            p.op("vector", lambda e, c=c: e.tensor_tensor(
                rchunk(c), rchunk(c), t["pmub"][:, 0:T], ALU.subtract))
            p.op("vector", lambda e, c=c: e.tensor_tensor(
                rchunk(c), rchunk(c), t["rstdb"][:, 0:T], ALU.mult))
            p.dma(outchunk(c), rchunk(c))



TSEG = 256
NSEG = 2048 // TSEG
CH = 128


def alloc_l0_attn(nc, es, lnt):
    t = dict(lnt)
    sb = lambda name, shape, dt: es.enter_context(nc.sbuf_tensor(name, shape, dt))
    ps = lambda name, shape, dt=F32: es.enter_context(nc.psum_tensor(name, shape, dt))
    T = TSEG
    t["stg"] = sb("z_stg", [128, 512], F32)
    t["win"] = sb("z_win", [128, 8 * 4384], BF16)
    t["wout"] = sb("z_wout", [128, 16 * 1024], BF16)
    t["normw"] = sb("z_normw", [128, 8], F32)
    t["normb"] = sb("z_normb", [128, 8], F32)
    t["convw"] = sb("z_convw", [128, 18 * 4], F32)
    t["convb"] = sb("z_convb", [128, 18], F32)
    t["dtbias"] = sb("z_dtbias", [32, 1], F32)
    t["acol"] = sb("z_acol", [32, 1], F32)
    t["dexp"] = sb("z_dexp", [128, 16], F32)
    t["rmsw"] = sb("z_rmsw", [128, 16], F32)
    t["esel"] = sb("z_esel", [32, 32 * 128], F32)
    t["selh"] = sb("z_selh", [32, 16 * 128], BF16)
    t["e127"] = sb("z_e127", [128, 128], BF16)
    t["mtril"] = sb("z_mtril", [128, 128], F32)
    t["maskbf"] = sb("z_maskbf", [128, 128], BF16)
    t["idf32"] = sb("z_idf32", [128, 128], F32)
    t["idbf"] = sb("z_idbf", [128, 128], BF16)
    t["xbf"] = sb("z_xbf", [128, 8 * T], BF16)
    t["hn"] = sb("z_hn", [128, 8 * T], BF16)
    t["zs"] = sb("z_zs", [128, 16 * T], BF16)
    t["xc"] = sb("z_xc", [128, 18 * (T + 3)], BF16)
    t["xba"] = sb("z_xba", [128, 18 * T], BF16)
    t["dtr"] = sb("z_dtr", [32, T], F32)
    t["yn"] = sb("z_yn", [128, 16 * T], BF16)
    t["rbf"] = sb("z_rbf", [128, 8 * T], BF16)
    t["cs1"] = t["sq"].ap()[:, 0:T]
    t["cs2"] = t["sq"].ap()[:, 256:256 + T]
    t["dtf"] = sb("z_dtf", [32, CH], F32)
    t["af"] = sb("z_af", [32, CH], F32)
    t["at32"] = sb("z_at32", [128, 32], F32)
    t["acfm"] = sb("z_acfm", [32, CH], F32)
    t["act_"] = sb("z_act", [128, 32], F32)
    t["eafm"] = sb("z_eafm", [32, CH], BF16)
    t["dsfm"] = sb("z_dsfm", [32, CH], BF16)
    t["eat"] = sb("z_eat", [128, 32], BF16)
    t["cdb"] = sb("z_cdb", [128, 32], F32)
    t["dst"] = sb("z_dst", [128, 32], F32)
    t["sm"] = sb("z_sm", [128, CH], BF16)
    t["lexp"] = [sb("z_lexp0", [128, CH], BF16), sb("z_lexp1", [128, CH], BF16)]
    t["g"] = [sb("z_g0", [128, CH], BF16), sb("z_g1", [128, CH], BF16)]
    t["btm"] = sb("z_btm", [128, CH], BF16)
    t["bds"] = [sb("z_bds0", [128, CH], BF16), sb("z_bds1", [128, CH], BF16)]
    t["xtm"] = sb("z_xtm", [128, 16 * CH], BF16)
    t["xfm"] = [sb("z_xfm0", [128, CH], BF16)] * 2
    t["eab"] = [sb("z_eab0", [128, CH], BF16)] * 2
    t["t1"] = [sb("z_t10", [128, CH], BF16)] * 2
    t["t2"] = [sb("z_t20", [128, CH], BF16), sb("z_t21", [128, CH], BF16)]
    t["prev"] = sb("z_prev", [128, 16 * CH], BF16)
    t["larg"] = [sb("z_larg0", [128, CH], F32)] * 2
    t["sp1"] = t["sq"].ap()[0:32, 256:384]
    t["sp2"] = t["sq"].ap()[0:32, 384:512]
    t["pM0"] = ps("z_pM0", [128, 512])
    t["pM1"] = ps("z_pM1", [128, 512])
    t["pR"] = [ps("z_pR0", [128, 128])] * 2
    t["pYY"] = [ps("z_pYY0", [128, 256]), ps("z_pYY1", [128, 256])]
    t["pS"] = ps("z_pS", [128, 64])
    t["pE"] = ps("z_pE", [128, 128])
    t["pXT"] = ps("z_pXT", [128, 256], BF16)
    t["pstat0"] = t["pM1"].ap()[0:1, 0:T]
    t["pstat1"] = t["pM1"].ap()[0:1, 256:256 + T]
    t["pmub"] = t["pM0"].ap()[:, 0:T]
    t["pbr"] = t["pM0"].ap()[:, 256:256 + T]
    return t


def load_l0_attn_weights(p, t, w):
    load_weight_bf16(p, t["stg"], t["win"], w["in_w"], 8, 4384)
    load_weight_bf16(p, t["stg"], t["wout"], w["out_w"], 16, 1024)
    for nm in ("normw", "normb", "convw", "convb", "dtbias", "dexp", "rmsw",
               "esel", "selh", "e127", "mtril", "maskbf", "idf32", "idbf"):
        p.dma(t[nm][:], w[nm])
    p.dma(t["stg"][0:32, 0:1], w["alog"])
    p.op("scalar", lambda e: e.activation(t["acol"][:], t["stg"][0:32, 0:1], AF.Exp))
    p.op("vector", lambda e: e.tensor_scalar(t["acol"][:], t["acol"][:], -1.0, None, ALU.mult))
    p.op("vector", lambda e: e.memset(t["eps"][:], 1e-5))


def emit_l0_attn_seq(p, t, orow, ocol, x_in, x_out, final_fp32, in_fp32=False):
    T = TSEG
    p.op("vector", lambda e: e.memset(t["prev"][:], 0.0))
    for dc in range(18):
        p.op("vector", lambda e, dc=dc: e.memset(t["xc"][:, dc * (T + 3): dc * (T + 3) + 3], 0.0))
    for seg in range(NSEG):
        tok = slice(seg * T, (seg + 1) * T)
        for c in range(8):
            if in_fp32:
                p.dma(t["stg"][:, 0:T], x_in[c, :, tok])
                p.op("vector", lambda e, c=c: e.tensor_copy(
                    t["xbf"][:, c * T:(c + 1) * T], t["stg"][:, 0:T]))
            else:
                p.dma(t["xbf"][:, c * T:(c + 1) * T], x_in[c, :, tok])
        emit_preln_l0(p, t, orow, ocol, T)
        for out_c in range(35):
            pb = t["pM0"] if out_c % 2 == 0 else t["pM1"]
            np_part = 32 if out_c == 34 else 128
            for k in range(8):
                p.op("tensor", lambda e, out_c=out_c, k=k, pb=pb, np_part=np_part: e.matmul(
                    pb[0:np_part, 0:T],
                    t["win"][:, k * 4384 + out_c * 128: k * 4384 + out_c * 128 + np_part],
                    t["hn"][:, k * T:(k + 1) * T], start=(k == 0), stop=(k == 7)))
            if out_c < 16:
                p.op("scalar", lambda e, out_c=out_c, pb=pb: e.activation(
                    t["zs"][:, out_c * T:(out_c + 1) * T], pb[:, 0:T], AF.Silu))
            elif out_c < 34:
                dc = out_c - 16
                p.op("scalar", lambda e, dc=dc, pb=pb: e.copy(
                    t["xc"][:, dc * (T + 3) + 3: dc * (T + 3) + 3 + T], pb[:, 0:T]))
            else:
                p.op("vector", lambda e, pb=pb: e.tensor_copy(t["dtr"][:, 0:T], pb[0:32, 0:T]))
        for dc in range(18):
            base = dc * (T + 3)
            p.op("vector", lambda e, dc=dc, base=base: e.tensor_scalar(
                t["cs1"][:], t["xc"][:, base: base + T], t["convw"][:, dc * 4: dc * 4 + 1], None, ALU.mult))
            for k2 in range(1, 4):
                p.op("vector", lambda e, dc=dc, base=base, k2=k2: e.tensor_scalar(
                    t["cs2"][:], t["xc"][:, base + k2: base + k2 + T], t["convw"][:, dc * 4 + k2: dc * 4 + k2 + 1], None, ALU.mult))
                p.op("vector", lambda e: e.tensor_tensor(t["cs1"][:], t["cs1"][:], t["cs2"][:], ALU.add))
            p.op("scalar", lambda e, dc=dc: e.activation(
                t["xba"][:, dc * T:(dc + 1) * T], t["cs1"][:], AF.Silu, bias=t["convb"][:, dc:dc + 1]))
            p.op("vector", lambda e, base=base: e.tensor_copy(
                t["xc"][:, base: base + 3], t["xc"][:, base + T: base + T + 3]))
        for cc in range(T // CH):
            emit_chunk(p, t, orow, ocol, cc)
        for oc in range(8):
            pb = t["pM0"] if oc % 2 == 0 else t["pM1"]
            for k in range(16):
                p.op("tensor", lambda e, oc=oc, k=k, pb=pb: e.matmul(
                    pb[:, 0:T], t["wout"][:, k * 1024 + oc * 128: k * 1024 + oc * 128 + 128],
                    t["yn"][:, k * T:(k + 1) * T], start=(k == 0), stop=(k == 15)))
            p.op("vector", lambda e, oc=oc, pb=pb: e.scalar_tensor_tensor(
                t["rbf"][:, oc * T:(oc + 1) * T], t["xbf"][:, oc * T:(oc + 1) * T], 2.0,
                pb[:, 0:T], ALU.mult, ALU.add))
        emit_postln2(p, t, orow, ocol,
                     lambda c: t["rbf"][:, c * T:(c + 1) * T],
                     lambda c, tok=tok: x_out[c, :, tok],
                     8, T, final_fp32)


def emit_chunk(p, t, orow, ocol, cc):
    T = TSEG
    col = cc * CH
    p.op("vector", lambda e, col=col: e.tensor_scalar(
        t["dtf"][:], t["dtr"][:, col:col + CH], t["dtbias"][:, 0:1], None, ALU.add))
    p.op("scalar", lambda e: e.activation(t["sp1"][:], t["dtf"][:], AF.Abs))
    p.op("scalar", lambda e: e.activation(t["sp1"][:], t["sp1"][:], AF.Exp, scale=-1.0))
    p.op("scalar", lambda e: e.activation(t["sp1"][:], t["sp1"][:], AF.Ln, bias=1.0))
    p.op("vector", lambda e: e.tensor_relu(t["sp2"][:], t["dtf"][:]))
    p.op("vector", lambda e: e.tensor_tensor(t["dtf"][:], t["sp1"][:], t["sp2"][:], ALU.add))
    p.op("vector", lambda e: e.tensor_copy(t["sp1"][:], t["dtf"][:]))
    p.op("vector", lambda e: e.tensor_scalar(t["af"][:], t["dtf"][:], t["acol"][:, 0:1], None, ALU.mult))
    p.op("tensor", lambda e: e.transpose(t["pYY"][0][:, 0:32], t["af"][:], t["idf32"][0:32, 0:32]))
    p.op("vector", lambda e: e.tensor_copy(t["at32"][:], t["pYY"][0][:, 0:32]))
    p.op("tensor", lambda e: e.matmul(t["pYY"][0][:, 32:64], t["mtril"][:], t["at32"][:], start=True, stop=True))
    p.op("vector", lambda e: e.tensor_copy(t["act_"][:], t["pYY"][0][:, 32:64]))
    p.op("tensor", lambda e: e.matmul(t["pYY"][1][0:32, 0:128], t["at32"][:], t["mtril"][:], start=True, stop=True))
    p.op("vector", lambda e: e.tensor_copy(t["acfm"][:], t["pYY"][1][0:32, 0:128]))
    p.op("scalar", lambda e: e.activation(t["eafm"][:], t["acfm"][:], AF.Exp))
    p.op("vector", lambda e: e.tensor_scalar(t["af"][:], t["acfm"][:], t["acfm"].ap()[:, CH - 1:CH], None, ALU.subtract))
    p.op("scalar", lambda e: e.activation(t["dsfm"][:], t["af"][:], AF.Exp, scale=-1.0))
    p.op("scalar", lambda e: e.activation(t["eat"][:], t["act_"][:], AF.Exp))
    p.op("tensor", lambda e: e.matmul(t["pE"][:, 0:32], t["e127"][:], t["eat"][:], start=True, stop=True))
    p.op("vector", lambda e: e.tensor_copy(t["cdb"][:], t["pE"][:, 0:32]))
    p.op("tensor", lambda e: e.transpose(t["pXT"][:, 128:160], t["dsfm"][:], t["idbf"][0:32, 0:32]))
    p.op("vector", lambda e: e.tensor_copy(t["dst"][:], t["pXT"][:, 128:160]))
    bsl = slice(16 * T + col, 16 * T + col + CH)
    csl = slice(17 * T + col, 17 * T + col + CH)
    p.op("tensor", lambda e, bsl=bsl, csl=csl: e.matmul(
        t["pR"][0][:], t["xba"][:, bsl], t["xba"][:, csl], start=True, stop=True))
    p.op("vector", lambda e: e.tensor_tensor(t["sm"][:], t["pR"][0][:], t["maskbf"][:], ALU.mult))
    p.op("tensor", lambda e, bsl=bsl: e.transpose(t["pXT"][:, 0:128], t["xba"][:, bsl], t["idbf"][:]))
    p.op("vector", lambda e: e.tensor_copy(t["btm"][:], t["pXT"][:, 0:128]))
    for j in range(16):
        xsl = slice(j * T + col, j * T + col + CH)
        xfm = t["xfm"][j % 2]
        p.op("tensor", lambda e, j=j: e.matmul(
            t["pE"][:], t["selh"][:, j * 128:(j + 1) * 128], t["sp1"][:], start=True, stop=True))
        p.op("vector", lambda e, xsl=xsl, xfm=xfm: e.tensor_tensor(
            xfm[:], t["xba"][:, xsl], t["pE"][:], ALU.mult))
        p.op("tensor", lambda e, j=j, xfm=xfm: e.transpose(t["pXT"][:, (j % 2) * 128:(j % 2) * 128 + 128], xfm[:], t["idbf"][:]))
        p.op("vector", lambda e, j=j: e.tensor_copy(t["xtm"][:, j * 128:(j + 1) * 128], t["pXT"][:, (j % 2) * 128:(j % 2) * 128 + 128]))
    for h in range(32):
        j = h // 2
        half = h % 2
        jp = j % 2
        psl = slice(half * 64, half * 64 + 64)
        pR = t["pR"][half]
        pYY = t["pYY"][jp]
        larg = t["larg"][half]
        lexp = t["lexp"][half]
        g = t["g"][half]
        bds = t["bds"][half]
        p.op("tensor", lambda e, h=h, pR=pR: e.matmul(
            pR[:], t["esel"][:, h * 128:(h + 1) * 128], t["acfm"][:], start=True, stop=True))
        p.op("vector", lambda e, h=h, larg=larg, pR=pR: e.tensor_scalar(
            larg[:], pR[:], t["act_"].ap()[:, h:h + 1], 0.0, ALU.subtract, ALU.min))
        p.op("scalar", lambda e, larg=larg, lexp=lexp: e.activation(lexp[:], larg[:], AF.Exp))
        p.op("vector", lambda e, lexp=lexp, g=g: e.tensor_tensor(g[:], t["sm"][:], lexp[:], ALU.mult))
        p.op("tensor", lambda e, h=h, psl=psl, g=g, pYY=pYY: e.matmul(
            pYY[psl, 0:128], t["xtm"][:, h * 64:(h + 1) * 64], g[:], start=True, stop=True))
        p.op("tensor", lambda e, h=h, psl=psl, csl=csl, pYY=pYY: e.matmul(
            pYY[psl, 128:256], t["prev"][:, h * 64:(h + 1) * 64], t["xba"][:, csl], start=True, stop=True))
        p.op("vector", lambda e, h=h, bds=bds: e.tensor_scalar(
            bds[:], t["btm"][:], t["dst"].ap()[:, h:h + 1], None, ALU.mult))
        p.op("tensor", lambda e, h=h, bds=bds: e.matmul(
            t["pS"][:], bds[:], t["xtm"][:, h * 64:(h + 1) * 64], start=True, stop=True))
        p.op("vector", lambda e, h=h: e.scalar_tensor_tensor(
            t["prev"][:, h * 64:(h + 1) * 64], t["prev"][:, h * 64:(h + 1) * 64],
            t["cdb"].ap()[:, h:h + 1], t["pS"][:], ALU.mult, ALU.add))
        if half == 1:
            eab = t["eab"][jp]
            t1 = t["t1"][jp]
            t2 = t["t2"][jp]
            p.op("tensor", lambda e, j=j: e.matmul(
                t["pE"][:], t["selh"][:, j * 128:(j + 1) * 128], t["eafm"][:], start=True, stop=True))
            p.op("vector", lambda e, eab=eab: e.tensor_copy(eab[:], t["pE"][:]))
            p.op("vector", lambda e, t1=t1, eab=eab, pYY=pYY: e.tensor_tensor(t1[:], pYY[:, 128:256], eab[:], ALU.mult))
            p.op("vector", lambda e, t1=t1, t2=t2, pYY=pYY: e.tensor_tensor(t2[:], t1[:], pYY[:, 0:128], ALU.add))
            xsl = slice(j * T + col, j * T + col + CH)
            p.op("vector", lambda e, j=j, xsl=xsl, t2=t2: e.scalar_tensor_tensor(
                t2[:], t["xba"][:, xsl], t["dexp"].ap()[:, j:j + 1], t2[:], ALU.mult, ALU.add))
            p.op("vector", lambda e, j=j, xsl=xsl, t2=t2: e.tensor_tensor(
                t2[:], t2[:], t["zs"][:, xsl], ALU.mult))
            p.op("vector", lambda e, j=j, xsl=xsl, t2=t2: e.tensor_copy(t["yn"][:, xsl], t2[:]))
    for j in range(16):
        xsl = slice(j * T + col, j * T + col + CH)
        p.op("scalar", lambda e, xsl=xsl: e.activation(t["sq"][:, 0:CH], t["yn"][:, xsl], AF.Square))
        p.op("tensor", lambda e, j=j: e.matmul(
            t["pM1"][0:1, 128:256], ocol[:, 0:1], t["sq"][:, 0:CH], start=(j == 0), stop=(j == 15)))
    R = t["rows"]
    p.op("vector", lambda e: e.tensor_scalar(R[:, 0:CH], t["pM1"][0:1, 128:256], 1.0 / 2048, None, ALU.mult))
    p.op("scalar", lambda e: e.activation(R[:, 512:512 + CH], R[:, 0:CH], AF.Ln, bias=t["eps"][0:1, :]))
    p.op("scalar", lambda e: e.activation(R[:, 512:512 + CH], R[:, 512:512 + CH], AF.Exp, scale=-0.5))
    p.op("vector", lambda e: e.tensor_copy(t["rows_bf"][:, 0:CH], R[:, 512:512 + CH]))
    p.op("tensor", lambda e: e.matmul(t["pM0"][:, 384:512], orow[:, 0:128], t["rows_bf"][:, 0:CH], start=True, stop=True))
    p.op("scalar", lambda e: e.copy(t["rstdb"][:, 0:CH], t["pM0"][:, 384:512]))
    for j in range(16):
        xsl = slice(j * T + col, j * T + col + CH)
        p.op("vector", lambda e, j=j, xsl=xsl: e.tensor_scalar(
            t["yn"][:, xsl], t["yn"][:, xsl], t["rmsw"].ap()[:, j:j + 1], None, ALU.mult))
        p.op("vector", lambda e, xsl=xsl: e.tensor_tensor(
            t["yn"][:, xsl], t["yn"][:, xsl], t["rstdb"][:, 0:CH], ALU.mult))


def emit_preln_l0(p, t, orow, ocol, T):
    for c in range(8):
        p.op("tensor", lambda e, c=c: e.matmul(
            t["pstat0"][:, 0:T], ocol[:, 0:1], t["xbf"][:, c * T:(c + 1) * T],
            start=(c == 0), stop=(c == 7)))
    for c in range(8):
        p.op("scalar", lambda e, c=c: e.activation(
            t["sq"][:, 0:T], t["xbf"][:, c * T:(c + 1) * T], AF.Square))
        p.op("tensor", lambda e, c=c: e.matmul(
            t["pstat1"][:, 0:T], ocol[:, 0:1], t["sq"][:, 0:T],
            start=(c == 0), stop=(c == 7)))
    R = t["rows"]
    inv = 1.0 / 1024
    p.op("vector", lambda e: e.tensor_scalar(R[:, 0:T], t["pstat0"][:, 0:T], inv, None, ALU.mult))
    p.op("vector", lambda e: e.tensor_scalar(R[:, 512:512 + T], t["pstat1"][:, 0:T], inv, None, ALU.mult))
    p.op("vector", lambda e: e.tensor_tensor(R[:, 1024:1024 + T], R[:, 0:T], R[:, 0:T], ALU.mult))
    p.op("vector", lambda e: e.tensor_tensor(R[:, 1024:1024 + T], R[:, 512:512 + T], R[:, 1024:1024 + T], ALU.subtract))
    p.op("scalar", lambda e: e.activation(R[:, 1536:1536 + T], R[:, 1024:1024 + T], AF.Ln, bias=t["eps"][0:1, :]))
    p.op("scalar", lambda e: e.activation(R[:, 1536:1536 + T], R[:, 1536:1536 + T], AF.Exp, scale=-0.5))
    p.op("vector", lambda e: e.tensor_copy(t["rows_bf"][:, 0:T], R[:, 0:T]))
    p.op("vector", lambda e: e.tensor_copy(t["rows_bf"][:, 512:512 + T], R[:, 1536:1536 + T]))
    p.op("tensor", lambda e: e.matmul(t["pmub"][:, 0:T], orow[:, 0:128], t["rows_bf"][:, 0:T], start=True, stop=True))
    p.op("tensor", lambda e: e.matmul(t["pbr"][:, 0:T], orow[:, 0:128], t["rows_bf"][:, 512:512 + T], start=True, stop=True))
    p.op("scalar", lambda e: e.copy(t["rstdb"][:, 0:T], t["pbr"][:, 0:T]))
    for c in range(8):
        p.op("vector", lambda e, c=c: e.tensor_tensor(
            t["hn"][:, c * T:(c + 1) * T], t["xbf"][:, c * T:(c + 1) * T], t["pmub"][:, 0:T], ALU.subtract))
        p.op("vector", lambda e, c=c: e.tensor_tensor(
            t["hn"][:, c * T:(c + 1) * T], t["hn"][:, c * T:(c + 1) * T], t["rstdb"][:, 0:T], ALU.mult))
        p.op("vector", lambda e, c=c: e.tensor_scalar(
            t["hn"][:, c * T:(c + 1) * T], t["hn"][:, c * T:(c + 1) * T],
            t["normw"][:, c:c + 1], t["normb"][:, c:c + 1], ALU.mult, ALU.add))


def host_prep_l0_attn(norm_w, norm_b, in_proj_w, conv_w, conv_b, dt_bias, A_log, D,
                      rms_w, out_proj_w):
    import ml_dtypes
    ident = np.eye(128, dtype=np.float32)
    esel = np.zeros((32, 32 * 128), dtype=np.float32)
    for h in range(32):
        esel[h, h * 128:(h + 1) * 128] = 1.0
    selh = np.zeros((32, 16 * 128), dtype=np.float32)
    for j in range(16):
        selh[2 * j, j * 128: j * 128 + 64] = 1.0
        selh[2 * j + 1, j * 128 + 64: j * 128 + 128] = 1.0
    e127 = np.zeros((128, 128), dtype=np.float32)
    e127[127, :] = 1.0
    tril = (np.arange(128)[:, None] <= np.arange(128)[None, :]).astype(np.float32)
    return {
        "in_w": np.ascontiguousarray(in_proj_w.T.reshape(8, 128, 4384)),
        "out_w": np.ascontiguousarray(out_proj_w.T.reshape(16, 128, 1024)),
        "normw": np.ascontiguousarray(norm_w.reshape(8, 128).T),
        "normb": np.ascontiguousarray(norm_b.reshape(8, 128).T),
        "convw": np.ascontiguousarray(conv_w.reshape(18, 128, 4).transpose(1, 0, 2).reshape(128, 72)),
        "convb": np.ascontiguousarray(conv_b.reshape(18, 128).T),
        "dtbias": np.ascontiguousarray(dt_bias.reshape(32, 1)),
        "alog": np.ascontiguousarray(A_log.reshape(32, 1)),
        "dexp": np.ascontiguousarray(np.repeat(D, 64).reshape(16, 128).T),
        "rmsw": np.ascontiguousarray(rms_w.reshape(16, 128).T),
        "esel": esel,
        "selh": selh.astype(ml_dtypes.bfloat16),
        "e127": e127.astype(ml_dtypes.bfloat16),
        "mtril": tril,
        "maskbf": tril.astype(ml_dtypes.bfloat16),
        "idf32": ident,
        "idbf": ident.astype(ml_dtypes.bfloat16),
    }




def alloc_l0_ffn(nc, es, lnt):
    t = dict(lnt)
    sb = lambda name, shape, dt: es.enter_context(nc.sbuf_tensor(name, shape, dt))
    ps = lambda name, shape, dt=F32: es.enter_context(nc.psum_tensor(name, shape, dt))
    t["w1"] = sb("f_w1", [128, 8 * 4096], BF16)
    t["w2"] = sb("f_w2", [128, 32 * 1024], BF16)
    t["b1"] = sb("f_b1", [128, 32], F32)
    t["b2r"] = sb("f_b2r", [1, 8 * 128], BF16)
    t["b2r32"] = sb("f_b2r32", [1, 8 * 128], F32)
    t["stg"] = sb("f_stg", [128, 1024], F32)
    t["xbf"] = sb("f_xbf", [128, 8 * 512], BF16)
    t["hbf"] = sb("f_hbf", [128, 32 * 512], BF16)
    t["rbf"] = sb("f_rbf", [128, 8 * 512], BF16)
    t["ph"] = [ps("f_ph0", [128, 512]), ps("f_ph1", [128, 512])]
    t["po"] = [ps("f_po0", [128, 512]), ps("f_po1", [128, 512])]
    t["pstat0"] = t["po"][0].ap()[0:1, 0:512]
    t["pstat1"] = t["po"][1].ap()[0:1, 0:512]
    t["pmub"] = ps("f_pmub", [128, 512], F32)
    t["pbr"] = t["ph"][0].ap()[:, 0:512]
    return t


def load_l0_ffn_weights(p, t, w):
    load_weight_bf16(p, t["stg"], t["w1"], w["w1"], 8, 4096)
    load_weight_bf16(p, t["stg"], t["w2"], w["w2"], 32, 1024)
    p.dma(t["b1"][:], w["b1"])
    p.dma(t["b2r32"][:], w["b2r"])
    p.op("vector", lambda e: e.memset(t["eps"][:], 1e-5))
    p.op("vector", lambda e: e.tensor_copy(t["b2r"][:], t["b2r32"][:]))


def emit_l0_ffn(p, t, ones_row_bf, ones_col_bf, x_in, x_out, final_fp32):
    for ti in range(4):
        tok = slice(ti * 512, (ti + 1) * 512)
        for c in range(8):
            p.dma(t["xbf"][:, c * 512:(c + 1) * 512], x_in[c, :, tok])
        for hc in range(32):
            pb = t["ph"][hc % 2]
            for k in range(8):
                p.op("tensor", lambda e, pb=pb, hc=hc, k=k: e.matmul(
                    pb[:], t["w1"][:, k * 4096 + hc * 128: k * 4096 + hc * 128 + 128],
                    t["xbf"][:, k * 512:(k + 1) * 512], start=(k == 0), stop=(k == 7)))
            p.op("scalar", lambda e, pb=pb, hc=hc: e.activation(
                t["hbf"][:, hc * 512:(hc + 1) * 512], pb[:], AF.Gelu,
                bias=t["b1"][:, hc:hc + 1]))
        for oc in range(8):
            pb = t["po"][oc % 2]
            for k in range(32):
                p.op("tensor", lambda e, pb=pb, oc=oc, k=k: e.matmul(
                    pb[:], t["w2"][:, k * 1024 + oc * 128: k * 1024 + oc * 128 + 128],
                    t["hbf"][:, k * 512:(k + 1) * 512], start=(k == 0), stop=False))
            p.op("tensor", lambda e, pb=pb, oc=oc: e.matmul(
                pb[:], t["b2r"][:, oc * 128:(oc + 1) * 128],
                ones_row_bf[:, 0:512], start=False, stop=True))
            p.op("vector", lambda e, pb=pb, oc=oc: e.tensor_tensor(
                t["rbf"][:, oc * 512:(oc + 1) * 512],
                t["xbf"][:, oc * 512:(oc + 1) * 512], pb[:], ALU.add))
        from_postln2 = True
        emit_postln2(p, t, ones_row_bf, ones_col_bf,
                     lambda c: t["rbf"][:, c * 512:(c + 1) * 512],
                     lambda c, tok=tok: x_out[c, :, tok],
                     8, 512, final_fp32)


def host_prep_l0_ffn(w1, b1, w2, b2):
    import ml_dtypes
    return {
        "w1": np.ascontiguousarray(w1.reshape(8, 128, 4096)),
        "b1": np.ascontiguousarray(b1.reshape(32, 128).T),
        "w2": np.ascontiguousarray(w2.reshape(32, 128, 1024)),
        "b2r": np.ascontiguousarray(b2.reshape(1, 8 * 128)),
    }



TSEG = 256
NSEG = 2048 // TSEG


def alloc_l1_attn(nc, es, lnt):
    t = dict(lnt)
    sb = lambda name, shape, dt: es.enter_context(nc.sbuf_tensor(name, shape, dt))
    ps = lambda name, shape, dt=F32: es.enter_context(nc.psum_tensor(name, shape, dt))
    T = TSEG
    t["stg"] = sb("a_stg", [128, 1024], F32)
    t["win"] = sb("a_win", [128, 8 * 4096], BF16)
    t["wout"] = sb("a_wout", [128, 16 * 1024], BF16)
    t["wxp"] = sb("a_wxp", [128, 16 * 96], BF16)
    t["wdtp"] = sb("a_wdtp", [64, 2048], BF16)
    t["convw"] = sb("a_convw", [128, 16 * 4], F32)
    t["convb"] = sb("a_convb", [128, 16], F32)
    t["dtpb"] = sb("a_dtpb", [128, 16], F32)
    t["aall"] = sb("a_aall", [128, 256], F32)
    t["dcol"] = sb("a_dcol", [128, 16], F32)
    t["lnw"] = sb("a_lnw", [128, 8], F32)
    t["lnb"] = sb("a_lnb", [128, 8], F32)
    t["inb"] = sb("a_inb", [128, 32], F32)
    t["outbr"] = sb("a_outbr", [1, 1024], BF16)
    t["outbr32"] = sb("a_outbr32", [1, 1024], F32)
    t["xbf"] = sb("a_xbf", [128, 8 * T], BF16)
    t["hn"] = sb("a_hn", [128, 8 * T], BF16)
    t["xc"] = sb("a_xc", [128, 16 * (T + 3)], BF16)
    t["gate"] = sb("a_gate", [128, 16 * T], BF16)
    t["xconv"] = sb("a_xconv", [128, 16 * T], BF16)
    t["dt"] = sb("a_dt", [128, 16 * T], BF16)
    t["u"] = sb("a_u", [128, 16 * T], BF16)
    t["yacc"] = sb("a_yacc", [128, 16 * T], BF16)
    t["dtraw"] = sb("a_dtraw", [64, T], BF16)
    t["bfm"] = sb("a_bfm", [16, T], BF16)
    t["cfm"] = sb("a_cfm", [16, T], BF16)
    t["decay"] = [sb("a_decay0", [128, T], BF16), sb("a_decay1", [128, T], BF16)]
    t["ub"] = [sb("a_ub0", [128, T], BF16), sb("a_ub1", [128, T], BF16)]
    t["straj"] = [sb("a_straj0", [128, T], BF16), sb("a_straj1", [128, T], BF16)]
    t["yp"] = [sb("a_yp0", [128, T], BF16), sb("a_yp1", [128, T], BF16)]
    t["bbs"] = [sb("a_bbs0", [128, T], BF16), sb("a_bbs1", [128, T], BF16)]
    t["cbs"] = [sb("a_cbs0", [128, T], BF16), sb("a_cbs1", [128, T], BF16)]
    t["state"] = sb("a_state", [128, 256], F32)
    t["rbf"] = sb("a_rbf", [128, 8 * T], BF16)
    t["cscr"] = [sb("a_cscr0", [128, T], BF16), sb("a_cscr1", [128, T], BF16)]
    t["cscr2"] = [sb("a_cscr20", [128, T], BF16), sb("a_cscr21", [128, T], BF16)]
    t["pM0"] = ps("a_pM0", [128, 512])
    t["pM1"] = ps("a_pM1", [128, 512])
    t["pP1"] = ps("a_pP1", [64, 512])
    t["pP2"] = ps("a_pP2", [16, 512])
    t["pP3"] = ps("a_pP3", [16, 512])
    t["pBC"] = ps("a_pBC", [128, 1024])
    t["pstat0"] = t["pP2"].ap()[0:1, :]
    t["pstat1"] = t["pP3"].ap()[0:1, :]
    t["pmub"] = t["pM0"].ap()[:, 0:T]
    t["pbr"] = t["pM0"].ap()[:, 256:256 + T]
    return t


def load_l1_attn_weights(p, t, w):
    load_weight_bf16(p, t["stg"], t["win"], w["in_w"], 8, 4096)
    load_weight_bf16(p, t["stg"], t["wout"], w["out_w"], 16, 1024)
    load_weight_bf16(p, t["stg"], t["wxp"], w["xp_w"], 16, 96)
    for hh in range(2):
        p.dma(t["stg"][0:64, 0:1024], w["dtp_w"][:, hh * 1024:(hh + 1) * 1024])
        p.op("vector", lambda e, hh=hh: e.tensor_copy(
            t["wdtp"][:, hh * 1024:(hh + 1) * 1024], t["stg"][0:64, 0:1024]))
    for nm in ("convw", "convb", "dtpb", "dcol", "lnw", "lnb", "inb"):
        p.dma(t[nm][:], w[nm])
    p.dma(t["outbr32"][:], w["outbr"])
    p.op("vector", lambda e: e.tensor_copy(t["outbr"][:], t["outbr32"][:]))
    p.dma(t["stg"][:, 0:256], w["alog"])
    p.op("scalar", lambda e: e.activation(t["aall"][:], t["stg"][:, 0:256], AF.Exp))
    p.op("vector", lambda e: e.tensor_scalar(t["aall"][:], t["aall"][:], -1.0, None, ALU.mult))
    p.op("vector", lambda e: e.memset(t["eps"][:], 1e-5))


def emit_l1_attn_seq(p, t, orow, ocol, sel16_sb, x_in, x_out, final_fp32):
    T = TSEG
    p.op("vector", lambda e: e.memset(t["state"][:], 0.0))
    for dc in range(16):
        p.op("vector", lambda e, dc=dc: e.memset(t["xc"][:, dc * (T + 3): dc * (T + 3) + 3], 0.0))
    for seg in range(NSEG):
        tok = slice(seg * T, (seg + 1) * T)
        for c in range(8):
            p.dma(t["xbf"][:, c * T:(c + 1) * T], x_in[c, :, tok])
        emit_ln_affine(p, t, orow, ocol, T)
        for out_c in range(32):
            pb = t["pM0"] if out_c % 2 == 0 else t["pM1"]
            for k in range(8):
                p.op("tensor", lambda e, out_c=out_c, k=k, pb=pb: e.matmul(
                    pb[:, 0:T], t["win"][:, k * 4096 + out_c * 128: k * 4096 + out_c * 128 + 128],
                    t["hn"][:, k * T:(k + 1) * T], start=(k == 0), stop=(k == 7)))
            if out_c < 16:
                dc = out_c
                p.op("scalar", lambda e, dc=dc, pb=pb: e.activation(
                    t["xc"][:, dc * (T + 3) + 3: dc * (T + 3) + 3 + T], pb[:, 0:T],
                    AF.Identity, bias=t["inb"][:, dc:dc + 1]))
            else:
                dc = out_c - 16
                p.op("scalar", lambda e, dc=dc, pb=pb, out_c=out_c: e.activation(
                    t["gate"][:, dc * T:(dc + 1) * T], pb[:, 0:T],
                    AF.Identity, bias=t["inb"][:, out_c:out_c + 1]))
        for dc in range(16):
            base = dc * (T + 3)
            cs = t["cscr"][dc % 2]
            cs2 = t["cscr2"][dc % 2]
            p.op("vector", lambda e, dc=dc, base=base, cs=cs: e.tensor_scalar(
                cs[:], t["xc"][:, base: base + T], t["convw"][:, dc * 4: dc * 4 + 1], None, ALU.mult))
            for k2 in range(1, 4):
                p.op("vector", lambda e, dc=dc, base=base, k2=k2, cs2=cs2: e.tensor_scalar(
                    cs2[:], t["xc"][:, base + k2: base + k2 + T], t["convw"][:, dc * 4 + k2: dc * 4 + k2 + 1], None, ALU.mult))
                p.op("vector", lambda e, cs=cs, cs2=cs2: e.tensor_tensor(cs[:], cs[:], cs2[:], ALU.add))
            p.op("scalar", lambda e, dc=dc, cs=cs: e.activation(
                t["xconv"][:, dc * T:(dc + 1) * T], cs[:], AF.Silu, bias=t["convb"][:, dc:dc + 1]))
            p.op("vector", lambda e, base=base: e.tensor_copy(
                t["xc"][:, base: base + 3], t["xc"][:, base + T: base + T + 3]))
        for k in range(16):
            p.op("tensor", lambda e, k=k: e.matmul(
                t["pP1"][:, 0:T], t["wxp"][:, k * 96: k * 96 + 64],
                t["xconv"][:, k * T:(k + 1) * T], start=(k == 0), stop=(k == 15)))
        for k in range(16):
            p.op("tensor", lambda e, k=k: e.matmul(
                t["pP2"][:, 0:T], t["wxp"][:, k * 96 + 64: k * 96 + 80],
                t["xconv"][:, k * T:(k + 1) * T], start=(k == 0), stop=(k == 15)))
        for k in range(16):
            p.op("tensor", lambda e, k=k: e.matmul(
                t["pP3"][:, 0:T], t["wxp"][:, k * 96 + 80: k * 96 + 96],
                t["xconv"][:, k * T:(k + 1) * T], start=(k == 0), stop=(k == 15)))
        p.op("vector", lambda e: e.tensor_copy(t["dtraw"][:, 0:T], t["pP1"][:, 0:T]))
        p.op("vector", lambda e: e.tensor_copy(t["bfm"][:, 0:T], t["pP2"][:, 0:T]))
        p.op("vector", lambda e: e.tensor_copy(t["cfm"][:, 0:T], t["pP3"][:, 0:T]))
        for dc in range(16):
            pb = t["pM0"] if dc % 2 == 0 else t["pM1"]
            cs = t["cscr"][dc % 2]
            cs2 = t["cscr2"][dc % 2]
            p.op("tensor", lambda e, dc=dc, pb=pb: e.matmul(
                pb[:, 0:T], t["wdtp"][:, dc * 128:(dc + 1) * 128], t["dtraw"][:, 0:T],
                start=True, stop=True))
            p.op("vector", lambda e, dc=dc, pb=pb, cs=cs: e.tensor_scalar(
                cs[:], pb[:, 0:T], t["dtpb"][:, dc:dc + 1], None, ALU.add))
            p.op("scalar", lambda e, cs=cs, cs2=cs2: e.activation(cs2[:], cs[:], AF.Abs))
            p.op("scalar", lambda e, cs2=cs2: e.activation(cs2[:], cs2[:], AF.Exp, scale=-1.0))
            p.op("scalar", lambda e, cs2=cs2: e.activation(cs2[:], cs2[:], AF.Ln, bias=1.0))
            p.op("vector", lambda e, cs=cs: e.tensor_relu(cs[:], cs[:]))
            p.op("vector", lambda e, dc=dc, cs=cs, cs2=cs2: e.tensor_tensor(
                t["dt"][:, dc * T:(dc + 1) * T], cs[:], cs2[:], ALU.add))
        for dc in range(16):
            p.op("vector", lambda e, dc=dc: e.tensor_tensor(
                t["u"][:, dc * T:(dc + 1) * T], t["xconv"][:, dc * T:(dc + 1) * T],
                t["dt"][:, dc * T:(dc + 1) * T], ALU.mult))
        pend_yacc = [None]
        it = 0
        for n in range(16):
            np_ = n % 2
            bbs = t["bbs"][np_]
            cbs = t["cbs"][np_]
            p.op("tensor", lambda e, n=n, np_=np_: e.matmul(
                t["pBC"][:, np_ * 512: np_ * 512 + T], sel16_sb[:, n * 128:(n + 1) * 128],
                t["bfm"][:, 0:T], start=True, stop=True))
            p.op("tensor", lambda e, n=n, np_=np_: e.matmul(
                t["pBC"][:, np_ * 512 + 256: np_ * 512 + 256 + T],
                sel16_sb[:, n * 128:(n + 1) * 128], t["cfm"][:, 0:T],
                start=True, stop=True))
            p.op("scalar", lambda e, bbs=bbs, np_=np_: e.copy(bbs[:, 0:T], t["pBC"][:, np_ * 512: np_ * 512 + T]))
            p.op("scalar", lambda e, cbs=cbs, np_=np_: e.copy(cbs[:, 0:T], t["pBC"][:, np_ * 512 + 256: np_ * 512 + 256 + T]))
            for dc in range(16):
                col = dc * 16 + n
                pp = it % 2
                it += 1
                dec = t["decay"][pp]
                ub = t["ub"][pp]
                straj = t["straj"][pp]
                yp = t["yp"][pp]
                p.op("scalar", lambda e, dc=dc, col=col, dec=dec: e.activation(
                    dec[:], t["dt"][:, dc * T:(dc + 1) * T], AF.Exp,
                    scale=t["aall"][:, col:col + 1]))
                p.op("gpsimd", lambda e, dc=dc, ub=ub, bbs=bbs: e.tensor_tensor(
                    ub[:], t["u"][:, dc * T:(dc + 1) * T], bbs[:, 0:T], ALU.mult))
                p.op("vector", lambda e, col=col, straj=straj, dec=dec, ub=ub: e.tensor_tensor_scan(
                    straj[:], dec[:], ub[:], t["state"][:, col:col + 1],
                    ALU.mult, ALU.add))
                p.op("vector", lambda e, col=col, straj=straj: e.tensor_copy(
                    t["state"][:, col:col + 1], straj[:, T - 1:T]))
                p.op("gpsimd", lambda e, straj=straj, cbs=cbs, yp=yp: e.tensor_tensor(
                    yp[:], straj[:], cbs[:, 0:T], ALU.mult))
                if pend_yacc[0] is not None:
                    pend_yacc[0]()
                def mk(n=n, dc=dc, yp=yp):
                    if n == 0:
                        return lambda: p.op("vector", lambda e, dc=dc, yp=yp: e.tensor_copy(
                            t["yacc"][:, dc * T:(dc + 1) * T], yp[:]))
                    return lambda: p.op("vector", lambda e, dc=dc, yp=yp: e.tensor_tensor(
                        t["yacc"][:, dc * T:(dc + 1) * T], t["yacc"][:, dc * T:(dc + 1) * T],
                        yp[:], ALU.add))
                pend_yacc[0] = mk()
        if pend_yacc[0] is not None:
            pend_yacc[0]()
            pend_yacc[0] = None
        for dc in range(16):
            p.op("vector", lambda e, dc=dc: e.scalar_tensor_tensor(
                t["yacc"][:, dc * T:(dc + 1) * T], t["xconv"][:, dc * T:(dc + 1) * T],
                t["dcol"][:, dc:dc + 1], t["yacc"][:, dc * T:(dc + 1) * T], ALU.mult, ALU.add))
            p.op("scalar", lambda e, dc=dc: e.activation(
                t["cscr"][dc % 2][:], t["gate"][:, dc * T:(dc + 1) * T], AF.Silu))
            p.op("vector", lambda e, dc=dc: e.tensor_tensor(
                t["yacc"][:, dc * T:(dc + 1) * T], t["yacc"][:, dc * T:(dc + 1) * T],
                t["cscr"][dc % 2][:], ALU.mult))
        for oc in range(8):
            pb = t["pM0"] if oc % 2 == 0 else t["pM1"]
            for k in range(16):
                p.op("tensor", lambda e, oc=oc, k=k, pb=pb: e.matmul(
                    pb[:, 0:T], t["wout"][:, k * 1024 + oc * 128: k * 1024 + oc * 128 + 128],
                    t["yacc"][:, k * T:(k + 1) * T], start=(k == 0), stop=False))
            p.op("tensor", lambda e, oc=oc, pb=pb: e.matmul(
                pb[:, 0:T], t["outbr"][:, oc * 128:(oc + 1) * 128], orow[:, 0:T],
                start=False, stop=True))
            p.op("vector", lambda e, oc=oc, pb=pb: e.scalar_tensor_tensor(
                t["rbf"][:, oc * T:(oc + 1) * T], t["xbf"][:, oc * T:(oc + 1) * T], 2.0,
                pb[:, 0:T], ALU.mult, ALU.add))
        emit_postln2(p, t, orow, ocol,
                     lambda c: t["rbf"][:, c * T:(c + 1) * T],
                     lambda c, tok=tok: x_out[c, :, tok],
                     8, T, final_fp32)


def emit_ln_affine(p, t, orow, ocol, T):
    for c in range(8):
        p.op("tensor", lambda e, c=c: e.matmul(
            t["pstat0"][:, 0:T], ocol[:, 0:1], t["xbf"][:, c * T:(c + 1) * T],
            start=(c == 0), stop=(c == 7)))
    for c in range(8):
        p.op("scalar", lambda e, c=c: e.activation(
            t["sq"][:, 0:T], t["xbf"][:, c * T:(c + 1) * T], AF.Square))
        p.op("tensor", lambda e, c=c: e.matmul(
            t["pstat1"][:, 0:T], ocol[:, 0:1], t["sq"][:, 0:T],
            start=(c == 0), stop=(c == 7)))
    R = t["rows"]
    inv = 1.0 / 1024
    p.op("vector", lambda e: e.tensor_scalar(R[:, 0:T], t["pstat0"][:, 0:T], inv, None, ALU.mult))
    p.op("vector", lambda e: e.tensor_scalar(R[:, 512:512 + T], t["pstat1"][:, 0:T], inv, None, ALU.mult))
    p.op("vector", lambda e: e.tensor_tensor(R[:, 1024:1024 + T], R[:, 0:T], R[:, 0:T], ALU.mult))
    p.op("vector", lambda e: e.tensor_tensor(R[:, 1024:1024 + T], R[:, 512:512 + T], R[:, 1024:1024 + T], ALU.subtract))
    p.op("scalar", lambda e: e.activation(R[:, 1536:1536 + T], R[:, 1024:1024 + T], AF.Ln, bias=t["eps"][0:1, :]))
    p.op("scalar", lambda e: e.activation(R[:, 1536:1536 + T], R[:, 1536:1536 + T], AF.Exp, scale=-0.5))
    p.op("vector", lambda e: e.tensor_copy(t["rows_bf"][:, 0:T], R[:, 0:T]))
    p.op("vector", lambda e: e.tensor_copy(t["rows_bf"][:, 512:512 + T], R[:, 1536:1536 + T]))
    p.op("tensor", lambda e: e.matmul(t["pmub"][:, 0:T], orow[:, 0:128], t["rows_bf"][:, 0:T], start=True, stop=True))
    p.op("tensor", lambda e: e.matmul(t["pbr"][:, 0:T], orow[:, 0:128], t["rows_bf"][:, 512:512 + T], start=True, stop=True))
    p.op("scalar", lambda e: e.copy(t["rstdb"][:, 0:T], t["pbr"][:, 0:T]))
    for c in range(8):
        p.op("vector", lambda e, c=c: e.tensor_tensor(
            t["hn"][:, c * T:(c + 1) * T], t["xbf"][:, c * T:(c + 1) * T], t["pmub"][:, 0:T], ALU.subtract))
        p.op("vector", lambda e, c=c: e.tensor_tensor(
            t["hn"][:, c * T:(c + 1) * T], t["hn"][:, c * T:(c + 1) * T], t["rstdb"][:, 0:T], ALU.mult))
        p.op("vector", lambda e, c=c: e.tensor_scalar(
            t["hn"][:, c * T:(c + 1) * T], t["hn"][:, c * T:(c + 1) * T],
            t["lnw"][:, c:c + 1], t["lnb"][:, c:c + 1], ALU.mult, ALU.add))


def host_prep_l1_attn(ln_w, ln_b, in_w, in_b, conv_w, conv_b, xp_w, dtp_w, dtp_b,
                      A_log, D, out_w, out_b):
    return {
        "in_w": np.ascontiguousarray(in_w.T.reshape(8, 128, 4096)),
        "out_w": np.ascontiguousarray(out_w.T.reshape(16, 128, 1024)),
        "xp_w": np.ascontiguousarray(xp_w.T.reshape(16, 128, 96)),
        "dtp_w": np.ascontiguousarray(dtp_w.T),
        "convw": np.ascontiguousarray(conv_w.reshape(16, 128, 4).transpose(1, 0, 2).reshape(128, 64)),
        "convb": np.ascontiguousarray(conv_b.reshape(16, 128).T),
        "dtpb": np.ascontiguousarray(dtp_b.reshape(16, 128).T),
        "alog": np.ascontiguousarray(A_log.reshape(16, 128, 16).transpose(1, 0, 2).reshape(128, 256)),
        "dcol": np.ascontiguousarray(D.reshape(16, 128).T),
        "lnw": np.ascontiguousarray(ln_w.reshape(8, 128).T),
        "lnb": np.ascontiguousarray(ln_b.reshape(8, 128).T),
        "inb": np.ascontiguousarray(in_b.reshape(32, 128).T),
        "outbr": np.ascontiguousarray(out_b.reshape(1, 1024)),
    }




def alloc_l1_ffn(nc, es, lnt):
    t = dict(lnt)
    sb = lambda name, shape, dt: es.enter_context(nc.sbuf_tensor(name, shape, dt))
    ps = lambda name, shape, dt=F32: es.enter_context(nc.psum_tensor(name, shape, dt))
    t["stg"] = sb("m_stg", [128, 2048], F32)
    t["w13"] = [sb("m_w13a", [128, 2 * 8 * 512], BF16), sb("m_w13b", [128, 2 * 8 * 512], BF16)]
    t["w2"] = [sb("m_w2a", [128, 4 * 1024], BF16), sb("m_w2b", [128, 4 * 1024], BF16)]
    t["wcv"] = sb("m_wcv", [128, 2048], BF16)
    t["rw"] = sb("m_rw", [128, 8 * 8], BF16)
    t["rbrow"] = sb("m_rbrow", [1, 128], F32)
    t["rbrow_bf"] = sb("m_rbrow_bf", [1, 128], BF16)
    t["xbf"] = sb("m_xbf", [128, 8 * 2048], BF16)
    t["gates"] = sb("m_gates", [8, 2048], BF16)
    t["ltm"] = sb("m_ltm", [128, 4 * 8], F32)
    t["lfm"] = sb("m_lfm", [8, 512], F32)
    t["ones8"] = sb("m_ones8", [128, 8], F32)
    t["tm"] = sb("m_tm", [128, 16 * 8], F32)
    t["cols"] = sb("m_cols", [128, 8], F32)
    t["gtm"] = sb("m_gtm", [128, 8], F32)
    t["s1"] = [sb("m_s1a", [128, 512], BF16), sb("m_s1b", [128, 512], BF16)]
    t["h1"] = [sb("m_h1a", [128, 4 * 512], BF16), sb("m_h1b", [128, 4 * 512], BF16)]
    t["rbf"] = sb("m_rbf", [128, 8 * 512], BF16)
    t["acc"] = sb("m_acc", [128, 4 * 8 * 256], F32)
    t["idf32"] = sb("m_idf32", [128, 128], F32)
    t["pA"] = [ps("m_pA0", [128, 512]), ps("m_pA1", [128, 512])]
    t["pO0"] = ps("m_pO0", [128, 2048])
    t["pT"] = ps("m_pT", [128, 512])
    t["pstat0"] = t["pT"].ap()[0:1, 0:512]
    t["pstat1"] = t["pO0"].ap()[0:1, 512:1024]
    t["pmub"] = t["pA"][0].ap()[:, 0:512]
    t["pbr"] = t["pA"][1].ap()[:, 0:512]
    return t


def load_l1_ffn_shared(p, t, w):
    p.dma(t["stg"].ap()[:, 0:64].rearrange("p (c e) -> p c e", c=8), w["router_w"].rearrange("c p e -> p c e"))
    p.op("vector", lambda e: e.tensor_copy(t["rw"][:, 0:64], t["stg"][:, 0:64]))
    p.op("vector", lambda e: e.memset(t["rbrow"][:], 0.0))
    p.dma(t["rbrow"][0:1, 0:8], w["router_b"])
    p.op("vector", lambda e: e.tensor_copy(t["rbrow_bf"][:], t["rbrow"][:]))
    p.op("vector", lambda e: e.memset(t["ones8"][:], 1.0))
    p.dma(t["idf32"][:], w["ident"])
    p.op("vector", lambda e: e.memset(t["eps"][:], 1e-5))


def emit_router(p, t, orow, x_seq_cols, T2=512):
    for ti in range(4):
        for k in range(8):
            p.op("tensor", lambda e, ti=ti, k=k: e.matmul(
                t["pA"][0][0:8, 0:T2], t["rw"][:, k * 8:(k + 1) * 8],
                t["xbf"][:, k * 2048 + ti * T2: k * 2048 + (ti + 1) * T2],
                start=(k == 0), stop=False))
        p.op("tensor", lambda e, ti=ti: e.matmul(
            t["pA"][0][0:8, 0:T2], t["rbrow_bf"][0:1, 0:8], orow[:, 0:T2],
            start=False, stop=True))
        p.op("scalar", lambda e: e.copy(t["lfm"][:], t["pA"][0][0:8, 0:T2]))
        for s in range(4):
            p.op("tensor", lambda e, s=s: e.transpose(
                t["pT"][:, 0:8], t["lfm"][:, s * 128:(s + 1) * 128], t["idf32"][0:8, 0:8]))
            p.op("vector", lambda e, s=s: e.tensor_copy(t["ltm"][:, s * 8:(s + 1) * 8], t["pT"][:, 0:8]))
        for s in range(4):
            lt = t["ltm"][:, s * 8:(s + 1) * 8]
            cols = t["cols"]
            tm = t["tm"]
            p.op("vector", lambda e, lt=lt: e.reduce_max(cols[:, 0:1], lt, AX.X))
            p.op("vector", lambda e, lt=lt: e.tensor_scalar(tm[:, 0:8], lt, cols[:, 0:1], None, ALU.is_equal))
            p.op("vector", lambda e: e.tensor_tensor_scan(tm[:, 8:16], t["ones8"][:], tm[:, 0:8], 0.0, ALU.mult, ALU.add))
            p.op("vector", lambda e: e.tensor_scalar(tm[:, 8:16], tm[:, 8:16], 1.0, None, ALU.is_equal))
            p.op("vector", lambda e: e.tensor_tensor(tm[:, 16:24], tm[:, 0:8], tm[:, 8:16], ALU.mult))
            p.op("vector", lambda e, lt=lt: e.scalar_tensor_tensor(tm[:, 24:32], tm[:, 16:24], -1e30, lt, ALU.mult, ALU.add))
            p.op("vector", lambda e: e.reduce_max(cols[:, 1:2], tm[:, 24:32], AX.X))
            p.op("vector", lambda e: e.tensor_scalar(tm[:, 32:40], tm[:, 24:32], cols[:, 1:2], None, ALU.is_equal))
            p.op("vector", lambda e: e.tensor_tensor_scan(tm[:, 40:48], t["ones8"][:], tm[:, 32:40], 0.0, ALU.mult, ALU.add))
            p.op("vector", lambda e: e.tensor_scalar(tm[:, 40:48], tm[:, 40:48], 1.0, None, ALU.is_equal))
            p.op("vector", lambda e: e.tensor_tensor(tm[:, 48:56], tm[:, 32:40], tm[:, 40:48], ALU.mult))
            p.op("vector", lambda e: e.tensor_tensor(cols[:, 2:3], cols[:, 1:2], cols[:, 0:1], ALU.subtract))
            p.op("scalar", lambda e: e.activation(cols[:, 3:4], cols[:, 2:3], AF.Exp))
            p.op("vector", lambda e: e.tensor_scalar(cols[:, 4:5], cols[:, 3:4], 1.0, None, ALU.add))
            p.op("vector", lambda e: e.reciprocal(cols[:, 5:6], cols[:, 4:5]))
            p.op("vector", lambda e: e.tensor_tensor(cols[:, 6:7], cols[:, 3:4], cols[:, 5:6], ALU.mult))
            p.op("vector", lambda e: e.tensor_scalar(t["gtm"][:], tm[:, 48:56], cols[:, 6:7], None, ALU.mult))
            p.op("vector", lambda e: e.scalar_tensor_tensor(t["gtm"][:], tm[:, 16:24], cols[:, 5:6], t["gtm"][:], ALU.mult, ALU.add))
            p.op("tensor", lambda e: e.transpose(t["pT"][0:8, 0:128], t["gtm"][:], t["idf32"][:]))
            p.op("scalar", lambda e, ti=ti, s=s: e.copy(
                t["gates"][0:8, ti * T2 + s * 128: ti * T2 + (s + 1) * 128], t["pT"][0:8, 0:128]))


def emit_moe_prepass(p, t, wr, scr):
    for ei in range(10):
        w1 = wr["rw1"][ei] if ei < 8 else wr["sw1"][ei - 8]
        w3 = wr["rw3"][ei] if ei < 8 else wr["sw3"][ei - 8]
        w2 = wr["rw2"][ei] if ei < 8 else wr["sw2"][ei - 8]
        for c in range(8):
            for src, coff in ((w1, 0), (w3, 8)):
                p.dma(t["stg"][:, 0:512], src[c])
                p.op("vector", lambda e: e.tensor_copy(t["wcv"][:, 0:512], t["stg"][:, 0:512]))
                p.dma(scr["s_w13"][ei, coff + c], t["wcv"][:, 0:512])
        for c in range(4):
            p.dma(t["stg"][:, 0:1024], w2[c])
            p.op("vector", lambda e: e.tensor_copy(t["wcv"][:, 0:1024], t["stg"][:, 0:1024]))
            p.dma(scr["s_w2"][ei, c], t["wcv"][:, 0:1024])


def emit_l1_ffn_seq(p, t, orow, ocol, sel8_sb, scr, x_in, x_out, final_fp32, dbg=None):
    for c in range(8):
        p.dma(t["xbf"][:, c * 2048:(c + 1) * 2048], x_in[c])
    emit_router(p, t, orow, None)
    T = 512
    for g in range(2):
        gcol = g * 1024
        for ei in range(10):
            par = ei % 2
            routed = ei < 8
            p.dma(t["w13"][par].ap().rearrange("p (c f) -> p c f", f=512), scr["s_w13"][ei].rearrange("c p f -> p c f"))
            p.dma(t["w2"][par].ap().rearrange("p (c f) -> p c f", f=1024), scr["s_w2"][ei].rearrange("c p f -> p c f"))
            for tt in range(2):
                col0 = gcol + tt * T
                w13 = t["w13"][par]
                w2 = t["w2"][par]
                h1 = t["h1"][tt % 2]
                for hc in range(4):
                    pA = t["pA"][hc % 2]
                    s1 = t["s1"][hc % 2]
                    for k in range(8):
                        p.op("tensor", lambda e, hc=hc, k=k, col0=col0, pA=pA, w13=w13: e.matmul(
                            pA[:, 0:T], w13[:, k * 512 + hc * 128: k * 512 + hc * 128 + 128],
                            t["xbf"][:, k * 2048 + col0: k * 2048 + col0 + T],
                            start=(k == 0), stop=(k == 7)))
                    p.op("scalar", lambda e, pA=pA, s1=s1: e.activation(s1[:, 0:T], pA[:, 0:T], AF.Silu))
                    for k in range(8):
                        p.op("tensor", lambda e, hc=hc, k=k, col0=col0, pA=pA, w13=w13: e.matmul(
                            pA[:, 0:T], w13[:, 4096 + k * 512 + hc * 128: 4096 + k * 512 + hc * 128 + 128],
                            t["xbf"][:, k * 2048 + col0: k * 2048 + col0 + T],
                            start=(k == 0), stop=(k == 7)))
                    p.op("vector", lambda e, hc=hc, pA=pA, s1=s1, h1=h1: e.tensor_tensor(
                        h1[:, hc * T:(hc + 1) * T], s1[:, 0:T], pA[:, 0:T], ALU.mult))
                if routed:
                    p.op("tensor", lambda e, ei=ei, col0=col0: e.matmul(
                        t["pT"][:, 0:T], sel8_sb[:, ei * 128:(ei + 1) * 128],
                        t["gates"][0:8, col0: col0 + T], start=True, stop=True))
                    for hc in range(4):
                        p.op("vector", lambda e, hc=hc, h1=h1: e.tensor_tensor(
                            h1[:, hc * T:(hc + 1) * T], h1[:, hc * T:(hc + 1) * T],
                            t["pT"][:, 0:T], ALU.mult))
                for oc in range(8):
                    pO = t["pO0"]
                    osl = slice((oc % 4) * 512, (oc % 4 + 1) * 512)
                    for k in range(4):
                        p.op("tensor", lambda e, oc=oc, k=k, pO=pO, osl=osl, w2=w2, h1=h1: e.matmul(
                            pO[:, osl], w2[:, k * 1024 + oc * 128: k * 1024 + oc * 128 + 128],
                            h1[:, k * 512:(k + 1) * 512],
                            start=(k == 0), stop=(k == 3)))
                    acsl = slice(tt * 8 * 512 + oc * 512, tt * 8 * 512 + (oc + 1) * 512)
                    if ei == 0:
                        p.op("vector", lambda e, pO=pO, osl=osl, acsl=acsl: e.tensor_copy(
                            t["acc"][:, acsl], pO[:, osl]))
                    else:
                        p.op("vector", lambda e, pO=pO, osl=osl, acsl=acsl: e.tensor_tensor(
                            t["acc"][:, acsl], t["acc"][:, acsl], pO[:, osl], ALU.add))
        for tt in range(2):
            col0 = gcol + tt * T
            for oc in range(8):
                acsl = slice(tt * 8 * 512 + oc * 512, tt * 8 * 512 + (oc + 1) * 512)
                p.op("vector", lambda e, oc=oc, col0=col0, acsl=acsl: e.tensor_tensor(
                    t["rbf"][:, oc * 512:(oc + 1) * 512],
                    t["xbf"][:, oc * 2048 + col0: oc * 2048 + col0 + T],
                    t["acc"][:, acsl], ALU.add))
            emit_postln2(p, t, orow, ocol,
                         lambda c: t["rbf"][:, c * 512:(c + 1) * 512],
                         lambda c, col0=col0: x_out[c, :, col0:col0 + T],
                         8, T, final_fp32)


def host_prep_l1_ffn(router_w, router_b, rw1, rw3, rw2, sw1, sw3, sw2):
    return {
        "router_w": np.ascontiguousarray(router_w.reshape(8, 128, 8)),
        "router_b": np.ascontiguousarray(router_b.reshape(1, 8)),
        "rw1": np.ascontiguousarray(rw1.reshape(8, 8, 128, 512)),
        "rw3": np.ascontiguousarray(rw3.reshape(8, 8, 128, 512)),
        "rw2": np.ascontiguousarray(rw2.reshape(8, 4, 128, 1024)),
        "sw1": np.ascontiguousarray(sw1.reshape(2, 8, 128, 512)),
        "sw3": np.ascontiguousarray(sw3.reshape(2, 8, 128, 512)),
        "sw2": np.ascontiguousarray(sw2.reshape(2, 4, 128, 1024)),
    }


from contextlib import ExitStack as _ExitStack

_BF16NP = ml_dtypes.bfloat16


def _ddt(arr):
    return BF16 if arr.dtype == _BF16NP else F32


def _mk_consts():
    import ml_dtypes
    return {
        "c_orow": np.ones((1, 512), dtype=ml_dtypes.bfloat16),
        "c_ocol": np.ones((128, 1), dtype=ml_dtypes.bfloat16),
        "c_ident": np.eye(128, dtype=np.float32),
        "c_sel16": _sel16(),
        "c_sel8": _sel8(),
    }


def _sel16():
    import ml_dtypes
    s = np.zeros((16, 16 * 128), dtype=ml_dtypes.bfloat16)
    for n in range(16):
        s[n, n * 128:(n + 1) * 128] = 1.0
    return s


def _sel8():
    import ml_dtypes
    s = np.zeros((8, 8 * 128), dtype=ml_dtypes.bfloat16)
    for n in range(8):
        s[n, n * 128:(n + 1) * 128] = 1.0
    return s


def _build_fused(w_a0, w_f0, w_a1, w_f1, mode="serial", plan=None):
    nc = bass.Bass("TRN2", target_bir_lowering=False, debug=False)
    xin = nc.dram_tensor("xin", [2, 8, 128, 2048], F32, kind="ExternalInput").ap()
    xout = nc.dram_tensor("xout", [2, 8, 128, 2048], F32, kind="ExternalOutput").ap()
    x1 = nc.dram_tensor("x1_scr", [2, 8, 128, 2048], BF16, kind="Internal").ap()
    x2 = nc.dram_tensor("x2_scr", [2, 8, 128, 2048], BF16, kind="Internal").ap()
    x3 = nc.dram_tensor("x3_scr", [2, 8, 128, 2048], BF16, kind="Internal").ap()

    def wdecl(prefix, wprep):
        return {k: nc.dram_tensor(prefix + k, list(v.shape), _ddt(v), kind="ExternalInput").ap()
                for k, v in wprep.items()}
    wd_a0 = wdecl("wa0_", w_a0)
    wd_f0 = wdecl("wf0_", w_f0)
    wd_a1 = wdecl("wa1_", w_a1)
    wd_f1 = wdecl("wf1_", w_f1)
    wd_f1["ident"] = nc.dram_tensor("c_ident", [128, 128], F32, kind="ExternalInput").ap()
    sel16_t = nc.dram_tensor("c_sel16", [16, 2048], BF16, kind="ExternalInput")
    sel8_t = nc.dram_tensor("c_sel8", [8, 1024], BF16, kind="ExternalInput")
    orow_t = nc.dram_tensor("c_orow", [1, 512], BF16, kind="ExternalInput")
    ocol_t = nc.dram_tensor("c_ocol", [128, 1], BF16, kind="ExternalInput")

    outer = _ExitStack()
    lnt = alloc_postln(nc, outer)
    orow = outer.enter_context(nc.sbuf_tensor("orow", [1, 512], BF16))
    ocol = outer.enter_context(nc.sbuf_tensor("ocol", [128, 1], BF16))
    sel16 = outer.enter_context(nc.sbuf_tensor("sel16", [16, 2048], BF16))
    sel8 = outer.enter_context(nc.sbuf_tensor("sel8", [8, 1024], BF16))
    sems = {e: outer.enter_context(nc.semaphore("sem_" + e)) for e in Prog.ENGINES}
    dqsems = [outer.enter_context(nc.semaphore("dq%d" % j)) for j in range(Prog.NDQ)]
    S = outer.enter_context(nc.semaphore("S"))
    block = outer.enter_context(nc.Block())
    p = Prog(nc, mode=mode, plan=plan)
    p.dma(orow[:], orow_t.ap())
    p.dma(ocol[:], ocol_t.ap())
    p.dma(sel16[:], sel16_t.ap())
    p.dma(sel8[:], sel8_t.ap())

    with _ExitStack() as es:
        t = alloc_l0_attn(nc, es, lnt)
        load_l0_attn_weights(p, t, wd_a0)
        for s in range(2):
            emit_l0_attn_seq(p, t, orow, ocol, xin[s], x1[s], final_fp32=False, in_fp32=True)
    with _ExitStack() as es:
        t = alloc_l0_ffn(nc, es, lnt)
        load_l0_ffn_weights(p, t, wd_f0)
        for s in range(2):
            emit_l0_ffn(p, t, orow, ocol, x1[s], x2[s], final_fp32=False)
    with _ExitStack() as es:
        t = alloc_l1_attn(nc, es, lnt)
        load_l1_attn_weights(p, t, wd_a1)
        for s in range(2):
            emit_l1_attn_seq(p, t, orow, ocol, sel16, x2[s], x3[s], final_fp32=False)
    scr = {"s_w13": nc.dram_tensor("moe_w13_scr", [10, 16, 128, 512], BF16, kind="Internal").ap(),
           "s_w2": nc.dram_tensor("moe_w2_scr", [10, 4, 128, 1024], BF16, kind="Internal").ap()}
    with _ExitStack() as es:
        t = alloc_l1_ffn(nc, es, lnt)
        load_l1_ffn_shared(p, t, wd_f1)
        emit_moe_prepass(p, t, wd_f1, scr)
        for s in range(2):
            emit_l1_ffn_seq(p, t, orow, ocol, sel8, scr, x3[s], xout[s], final_fp32=True)

    if mode == "harvest":
        p.emit_harvest(block)
    elif mode == "sched":
        p.emit_sched(block, sems, dqsems)
    else:
        p.emit(block, S)
    alloc = extract_alloc_map(nc) if mode == "harvest" else None
    outer.close()
    return nc, p, alloc


def kernel(**inputs):
    i = inputs
    consts = _mk_consts()
    w_a0 = host_prep_l0_attn(i["L0_attn_norm_w"], i["L0_attn_norm_b"], i["L0_attn_in_proj_w"],
                             i["L0_attn_conv_w"], i["L0_attn_conv_b"], i["L0_attn_dt_bias"],
                             i["L0_attn_A_log"], i["L0_attn_D"], i["L0_attn_rms_w"],
                             i["L0_attn_out_proj_w"])
    w_f0 = host_prep_l0_ffn(i["L0_ffn_w1"], i["L0_ffn_b1"], i["L0_ffn_w2"], i["L0_ffn_b2"])
    w_a1 = host_prep_l1_attn(i["L1_attn_ln_w"], i["L1_attn_ln_b"], i["L1_attn_in_w"],
                             i["L1_attn_in_b"], i["L1_attn_conv_w"], i["L1_attn_conv_b"],
                             i["L1_attn_xp_w"], i["L1_attn_dtp_w"], i["L1_attn_dtp_b"],
                             i["L1_attn_A_log"], i["L1_attn_D"], i["L1_attn_out_w"],
                             i["L1_attn_out_b"])
    w_f1 = host_prep_l1_ffn(i["L1_ffn_router_w"], i["L1_ffn_router_b"], i["L1_ffn_routed_w1"],
                            i["L1_ffn_routed_w3"], i["L1_ffn_routed_w2"], i["L1_ffn_shared_w1"],
                            i["L1_ffn_shared_w3"], i["L1_ffn_shared_w2"])
    x = i["L0_attn_x"]
    x_fm = np.ascontiguousarray(x.reshape(16, 2048, 8, 128).transpose(0, 2, 3, 1))
    import os
    if os.environ.get("KMODE", "sched") == "serial":
        nc, _, _ = _build_fused(w_a0, w_f0, w_a1, w_f1, mode="serial")
    else:
        nc1, p1, alloc = _build_fused(w_a0, w_f0, w_a1, w_f1, mode="harvest")
        plan = Prog.finalize_plan(p1.ops, Prog.compute_plan(p1.ops, p1.meta, alloc))
        del nc1, p1
        nc, _, _ = _build_fused(w_a0, w_f0, w_a1, w_f1, mode="sched", plan=plan)
    in_maps = []
    for c in range(8):
        m = {"xin": np.ascontiguousarray(x_fm[2 * c:2 * c + 2])}
        for pre, wp in (("wa0_", w_a0), ("wf0_", w_f0), ("wa1_", w_a1), ("wf1_", w_f1)):
            for k, v in wp.items():
                m[pre + k] = v
        for k, v in consts.items():
            m[k] = v
        in_maps.append(m)
    res = __run_spmd(nc, in_maps, core_ids=list(range(8)))
    out_fm = np.concatenate([r["xout"] for r in res.results], axis=0)
    out = out_fm.transpose(0, 3, 1, 2).reshape(16, 2048, 1024)
    return np.ascontiguousarray(out)